# version 04
# baseline (speedup 1.0000x reference)
import numpy as np
import concourse.bass as bass
import concourse.mybir as mybir
from concourse.bass_utils import run_bass_kernel_spmd
from contextlib import ExitStack

F32 = mybir.dt.float32
BF16 = mybir.dt.bfloat16
AF = mybir.ActivationFunctionType
ALU = mybir.AluOpType

B = 4
L = 4096
H = 2048
I = 4096
NOC = 64
NKT = 32
NN = 16
R = 128
ICF = 5632
NIC = 44
NHC = 16
LC = 512
NCH = L // LC
SUP = 1024
NSUP = L // SUP
C4 = SUP // LC
TFF = 512
NTF = L // TFF
KB = 4
NKB = NKT // KB
EPS = 1e-5

C_EPS, C_ONE, C_NEG1, C_NEGH, C_15, C_HALF, C_I2048, C_I128, C_I16, C_ZERO = range(10)
NCST = 10

STREAM_OF = {"pe": "tensor", "act": "scalar", "dve": "vector", "pool": "gpsimd",
             "sq": "sync", "aq": "scalar", "gq": "gpsimd"}
COMPUTE_ENGS = {"pe", "act", "dve", "pool"}
ENG_OF_STREAM = {"tensor": "pe", "scalar": "act", "vector": "dve",
                 "gpsimd": "pool", "sync": "sq"}


class Op:
    __slots__ = ("eng", "emit", "afters", "inc", "flag", "val", "seq", "key")

    def __init__(self, eng, emit, afters, inc, seq, key):
        self.eng = eng
        self.emit = emit
        self.afters = afters
        self.inc = inc
        self.flag = False
        self.val = None
        self.seq = seq
        self.key = key


def _flat(after):
    out = []
    for a in after:
        if a is None:
            continue
        if isinstance(a, (list, tuple)):
            out.extend(x for x in a if x is not None)
        else:
            out.append(a)
    return out


class Prog:

    def __init__(self):
        self.ops = []
        self.last_real = {}
        self.chan_last = {}

    def do(self, eng, emit, after=(), dma=False, chan=None):
        key = chan if dma else eng
        assert (chan is None) or dma
        if dma:
            assert chan is not None, "dma needs a channel"
        op = Op(eng, emit, _flat(after), 16 if dma else 1, len(self.ops), key)
        self.ops.append(op)
        if emit is not None:
            self.last_real[STREAM_OF[eng]] = op
            if dma:
                self.chan_last[chan] = op
        return op

    def barrier(self):
        lasts = list(self.last_real.items())
        chans = list(self.chan_last.values())
        for st in ("tensor", "scalar", "vector", "gpsimd", "sync"):
            afters = [op for s2, op in lasts if s2 != st] + chans
            if afters:
                self.do(ENG_OF_STREAM[st], None, after=afters)

    def finalize(self):
        keys = []
        seen = set()
        for op in self.ops:
            if op.key not in seen:
                seen.add(op.key)
                keys.append(op.key)
        for op in self.ops:
            for a in op.afters:
                a.flag = True
        counters = {k: 0 for k in keys}
        for op in self.ops:
            if op.flag:
                assert op.emit is not None, "cannot wait on a wait-only op"
            if op.flag or (op.inc == 16 and op.emit is not None):
                op.flag = True
                counters[op.key] += op.inc
                op.val = counters[op.key]
        per_stream = {s: [] for s in ("tensor", "scalar", "vector", "gpsimd", "sync")}
        for op in self.ops:
            per_stream[STREAM_OF[op.eng]].append(op)
        return per_stream, keys

    @staticmethod
    def emit_stream(eng_obj, stream, ops, sems, hw):
        for op in ops:
            for a in op.afters:
                if (STREAM_OF[a.eng] == stream and a.eng in COMPUTE_ENGS
                        and a.seq < op.seq):
                    continue
                if hw.get((stream, a.key), 0) >= a.val:
                    continue
                eng_obj.wait_ge(sems[a.key], a.val)
                hw[(stream, a.key)] = a.val
            if op.emit is not None:
                instr = op.emit(eng_obj)
                if op.flag:
                    instr.then_inc(sems[op.key], op.inc)


class Arena:

    def __init__(self, tensor, ncols_bf16):
        self.t = tensor
        self.n = ncols_bf16
        self.cur = 0

    def alloc(self, cols, dtype):
        units = cols * (2 if dtype == F32 else 1)
        units = (units + 31) & ~31
        assert self.cur + units <= self.n, f"arena overflow {self.cur}+{units}>{self.n}"
        ap = self.t[:, self.cur:self.cur + units]
        self.cur += units
        if dtype == F32:
            ap = ap.bitcast(F32)
        return ap[:, 0:cols]

    def mark(self):
        return self.cur

    def reset(self, mark):
        self.cur = mark


class Banks:
    def __init__(self, tensors):
        self.t = tensors
        self.w = [None] * len(tensors)
        self.r = [None] * len(tensors)


def build_nc(debug=False, paranoid=0):
    nc = bass.Bass("TRN2", target_bir_lowering=False)

    xT = nc.dram_tensor("xT", [H, L], F32, kind="ExternalInput")
    w_in_t = nc.dram_tensor("w_in_t", [NOC, 128, H], F32, kind="ExternalInput")
    wx_t = nc.dram_tensor("wx_t", [NKT, 128, 160], F32, kind="ExternalInput")
    wdt_t = nc.dram_tensor("wdt_t", [R, I], F32, kind="ExternalInput")
    wout_t = nc.dram_tensor("wout_t", [NHC, 128, I], F32, kind="ExternalInput")
    wg_t = nc.dram_tensor("wg_t", [NIC, 128, H], F32, kind="ExternalInput")
    wu_t = nc.dram_tensor("wu_t", [NIC, 128, H], F32, kind="ExternalInput")
    wd_t = nc.dram_tensor("wd_t", [NHC, 128, ICF], F32, kind="ExternalInput")
    convw_s = nc.dram_tensor("convw_s", [128, 128], F32, kind="ExternalInput")
    convb_s = nc.dram_tensor("convb_s", [128, 32], F32, kind="ExternalInput")
    dtb_s = nc.dram_tensor("dtb_s", [128, 32], F32, kind="ExternalInput")
    A_s = nc.dram_tensor("A_s", [128, 512], F32, kind="ExternalInput")
    D_s = nc.dram_tensor("D_s", [128, 32], F32, kind="ExternalInput")
    gs_s = nc.dram_tensor("gs_s", [128, NIC], F32, kind="ExternalInput")
    us_s = nc.dram_tensor("us_s", [128, NIC], F32, kind="ExternalInput")
    ds_s = nc.dram_tensor("ds_s", [128, NHC], F32, kind="ExternalInput")
    dtln_s = nc.dram_tensor("dtln_s", [128, 1], F32, kind="ExternalInput")
    bln_s = nc.dram_tensor("bln_s", [16, 1], F32, kind="ExternalInput")
    cln_s = nc.dram_tensor("cln_s", [16, 1], F32, kind="ExternalInput")
    cst_in = nc.dram_tensor("cst_in", [128, NCST], F32, kind="ExternalInput")

    o_out = nc.dram_tensor("o_out", [H, L], F32, kind="ExternalOutput")

    d_xn1 = nc.dram_tensor("d_xn1", [H, L], BF16)
    d_x1 = nc.dram_tensor("d_x1", [H, L], F32)
    d_xn2 = nc.dram_tensor("d_xn2", [H, L], BF16)
    d_hconv = nc.dram_tensor("d_hconv", [I, L], F32)
    d_gate = nc.dram_tensor("d_gate", [I, L], F32)
    d_bc = nc.dram_tensor("d_bc", [32, L], F32)
    d_x2 = nc.dram_tensor("d_x2", [H, L], F32)
    d_xn3 = nc.dram_tensor("d_xn3", [H, L], BF16)
    d_stat = nc.dram_tensor("d_stat", [4, L], F32)
    d_wg = nc.dram_tensor("d_wg", [NIC, 128, H], BF16)
    d_wu = nc.dram_tensor("d_wu", [NIC, 128, H], BF16)
    d_wd = nc.dram_tensor("d_wd", [NHC, 128, ICF], BF16)
    d_wout = nc.dram_tensor("d_wout", [NHC, 128, I], BF16)

    with ExitStack() as ctx:
        sb = lambda name, shape, dt_: ctx.enter_context(nc.sbuf_tensor(name, shape, dt_))
        cst = sb("cst", [128, NCST], F32)
        ones = sb("ones", [128, 1], F32)
        convw = sb("convw", [128, 128], F32)
        convb = sb("convb", [128, 32], F32)
        dtb = sb("dtb", [128, 32], F32)
        A_sb = sb("A_sb", [128, 512], F32)
        D_sb = sb("D_sb", [128, 32], F32)
        gs = sb("gs", [128, NIC], F32)
        us = sb("us", [128, NIC], F32)
        ds = sb("ds", [128, NHC], F32)
        hds = sb("hds", [128, NHC], F32)
        dtln = sb("dtln", [128, 1], F32)
        bln = sb("bln", [16, 1], F32)
        cln = sb("cln", [16, 1], F32)
        wx_bf = sb("wx_bf", [128, NKT * 160], BF16)
        wdt_bf = sb("wdt_bf", [128, I], BF16)
        ts_n = sb("ts_n", [128, L], BF16)
        state = sb("state", [128, NKT * NN], F32)
        halo = sb("halo", [128, 32 * 4], F32)
        s_ring = sb("s_ring", [128, KB * (LC + 1)], F32)
        stat_row = sb("stat_row", [4, 512], F32)
        stat2 = sb("stat2", [4, 512], F32)
        AR_N = 70656
        arena = Arena(sb("arena", [128, AR_N], BF16), AR_N)

        ps_t = [ctx.enter_context(nc.psum_tensor(f"ps{i}", [128, 512], F32))
                for i in range(8)]
        banks = Banks(ps_t)

        p = Prog()

        def ld(dst_ap, src_ap, after=(), q="sq", chan=None):
            return p.do(q, lambda e, d=dst_ap, s=src_ap: e.dma_start(out=d, in_=s),
                        after=after, dma=True, chan=chan)

        st_ = ld

        def mm(bank_i, lhsT, rhs, start, stop, after=(), base=0, m=128):
            out = banks.t[bank_i][base:base + m, :rhs.shape[-1]]
            extra = [banks.r[bank_i]] if start else []
            op = p.do("pe", lambda e, o=out, lh=lhsT, r=rhs, s1=start, s2=stop:
                      e.matmul(o, lh, r, start=s1, stop=s2),
                      after=list(after) + extra)
            banks.w[bank_i] = op
            return op

        def act(out, in_, func, bias=0.0, scale=1.0, after=()):
            return p.do("act", lambda e, o=out, i=in_, f=func, b=bias, s=scale:
                        e.activation(o, i, f, bias=b, scale=s), after=after)

        def tt(out, a, b_, aop, after=(), eng="dve"):
            return p.do(eng, lambda e, o=out, x=a, y=b_, op_=aop:
                        e.tensor_tensor(o, x, y, op_), after=after)

        def ts(out, a, s1, s2, op0, op1=None, after=(), eng="dve"):
            if op1 is None:
                return p.do(eng, lambda e, o=out, x=a, u=s1, v=s2, p0=op0:
                            e.tensor_scalar(o, x, u, v, p0), after=after)
            return p.do(eng, lambda e, o=out, x=a, u=s1, v=s2, p0=op0, p1=op1:
                        e.tensor_scalar(o, x, u, v, p0, p1), after=after)

        def stt(out, in0, scal, in1, op0, op1, after=(), eng="dve"):
            return p.do(eng, lambda e, o=out, x=in0, s=scal, y=in1, p0=op0, p1=op1:
                        e.scalar_tensor_tensor(o, x, s, y, p0, p1), after=after)

        def cp(out, in_, after=(), eng="dve"):
            return p.do(eng, lambda e, o=out, i=in_: e.tensor_copy(o, i), after=after)

        def memset(ap, v, after=(), eng="dve"):
            return p.do(eng, lambda e, a=ap, vv=v: e.memset(a, vv), after=after)

        def scan(out, d0, d1, init, after=()):
            return p.do("dve", lambda e, o=out, a=d0, b_=d1, i=init:
                        e.tensor_tensor_scan(o, a, b_, i, ALU.mult, ALU.add),
                        after=after)

        cEPS1 = cst[0:1, C_EPS:C_EPS + 1]
        cNEGH1 = cst[0:1, C_NEGH:C_NEGH + 1]
        c151 = cst[0:1, C_15:C_15 + 1]
        cI2048 = cst[0:1, C_I2048:C_I2048 + 1]
        cI128 = cst[0:1, C_I128:C_I128 + 1]
        cI16 = cst[0:1, C_I16:C_I16 + 1]

        stat_busy = [None]
        stat2_busy = [None]

        def rstd_chain(src_row, prescale, rows, cols, after=()):
            row_v, row_r0, row_t = rows
            o1 = ts(row_v[:, 0:cols], src_row, prescale, cEPS1,
                    ALU.mult, ALU.add, after=after)
            o2 = act(row_r0[:, 0:cols], row_v[:, 0:cols], AF.Ln, after=[o1])
            o3 = act(row_r0[:, 0:cols], row_r0[:, 0:cols], AF.Exp, scale=cNEGH1,
                     after=[o2])
            o4 = tt(row_t[:, 0:cols], row_r0[:, 0:cols], row_r0[:, 0:cols],
                    ALU.mult, after=[o3])
            o5 = tt(row_t[:, 0:cols], row_t[:, 0:cols], row_v[:, 0:cols],
                    ALU.mult, after=[o4, o1])
            o6 = ts(row_t[:, 0:cols], row_t[:, 0:cols], cNEGH1, c151,
                    ALU.mult, ALU.add, after=[o5])
            o7 = tt(stat_row[0:1, 0:cols], row_r0[:, 0:cols], row_t[:, 0:cols],
                    ALU.mult, after=[o6, o3, stat_busy[0]])
            return o7

        lds = []
        for dst, src in ((cst, cst_in), (convw, convw_s), (convb, convb_s),
                         (dtb, dtb_s), (A_sb, A_s), (D_sb, D_s), (gs, gs_s),
                         (us, us_s), (ds, ds_s), (dtln, dtln_s)):
            lds.append(ld(dst[:], src.ap(), chan="p0c"))
        lds.append(ld(bln[0:16, :], bln_s.ap(), chan="p0c"))
        lds.append(ld(cln[0:16, :], cln_s.ap(), chan="p0c"))
        wx_src = bass.AP(wx_t, 0, [[160, 128], [128 * 160, NKT], [1, 160]])
        lds.append(ld(wx_bf[:], wx_src, q="gq", chan="p0g"))
        lds.append(ld(wdt_bf[:], wdt_t.ap(), q="gq", chan="p0g"))

        memset(ones[:], 1.0)
        ts(hds[:], ds[:], cst[:, C_HALF:C_HALF + 1], None, ALU.mult, after=lds)
        memset(state[:], 0.0)
        memset(halo[:], 0.0)

        m0 = arena.mark()
        wc_f = [arena.alloc(8192, F32) for _ in range(2)]
        wc_b = [arena.alloc(8192, BF16) for _ in range(2)]
        cast_jobs = []
        for ic0 in range(0, NIC, 4):
            n = min(4, NIC - ic0)
            for (src, dst) in ((wg_t, d_wg), (wu_t, d_wu)):
                ap_pat = [[H, 128], [128 * H, n], [1, H]]
                cast_jobs.append((bass.AP(src, ic0 * 128 * H, ap_pat),
                                  bass.AP(dst, ic0 * 128 * H, ap_pat), n * H))
        for hc in range(NHC):
            ap_pat = [[ICF, 128], [1, ICF]]
            cast_jobs.append((bass.AP(wd_t, hc * 128 * ICF, ap_pat),
                              bass.AP(d_wd, hc * 128 * ICF, ap_pat), ICF))
        for ot in range(NHC):
            ap_pat = [[I, 128], [1, I]]
            cast_jobs.append((bass.AP(wout_t, ot * 128 * I, ap_pat),
                              bass.AP(d_wout, ot * 128 * I, ap_pat), I))
        cast_c = [None, None]
        cast_s = [None, None]
        for j, (srcap, dstap, cols) in enumerate(cast_jobs):
            bi = j % 2
            l_ = ld(wc_f[bi][:, 0:cols], srcap, after=[cast_c[bi]],
                    chan=f"cstl{bi}")
            c_ = act(wc_b[bi][:, 0:cols], wc_f[bi][:, 0:cols], AF.Copy,
                     after=[l_, cast_s[bi]])
            s_ = st_(dstap, wc_b[bi][:, 0:cols], after=[c_], chan=f"csts{bi}")
            cast_c[bi] = c_
            cast_s[bi] = s_
        p.barrier()
        arena.reset(m0)

        def ln_phase(src_dram, dst_dram):
            m0 = arena.mark()
            x_sb = [arena.alloc(8192, F32) for _ in range(2)]
            sq_sb = [arena.alloc(512, F32) for _ in range(2)]
            rows = [arena.alloc(512, F32)[0:1, :] for _ in range(3)]
            row_mu = arena.alloc(512, F32)[0:1, :]
            rstd_b = [arena.alloc(512, F32) for _ in range(2)]
            c_b = [arena.alloc(512, F32) for _ in range(2)]
            t1 = arena.alloc(512, F32)
            xn_sb = [arena.alloc(8192, BF16) for _ in range(2)]
            lu_x = [None, None]
            lu_bc = [None, None]
            lu_xn = [None, None]
            sqmm = [None, None]
            for t in range(NTF):
                bi = t % 2
                bk_x = 4 + bi
                bk_s = 6 + bi
                ldx = []
                for g in range(4):
                    srcap = bass.AP(src_dram, g * 4 * 128 * L + t * TFF,
                                    [[L, 128], [128 * L, 4], [1, TFF]])
                    ldx.append(ld(x_sb[bi][:, g * 2048:(g + 1) * 2048], srcap,
                                  after=[lu_x[bi]], chan=f"lnx{bi}"))
                mmx = None
                mms = None
                for hc in range(NHC):
                    xs = x_sb[bi][:, hc * 512:(hc + 1) * 512]
                    mmx = mm(bk_x, ones[:, 0:1], xs, start=(hc == 0), stop=(hc == NHC - 1),
                             after=ldx, base=0, m=1)
                    sq_ev = act(sq_sb[hc % 2], xs, AF.Square,
                                after=ldx + [sqmm[hc % 2]])
                    mms = mm(bk_s, ones[:, 0:1], sq_sb[hc % 2],
                             start=(hc == 0), stop=(hc == NHC - 1),
                             after=[sq_ev], base=0, m=1)
                    sqmm[hc % 2] = mms
                mu = act(row_mu, banks.t[bk_x][0:1, :], AF.Copy, scale=cI2048,
                         after=[mmx])
                bank_read(bk_x, mu)
                ex2 = act(rows[0][0:1, :], banks.t[bk_s][0:1, :], AF.Copy,
                          scale=cI2048, after=[mms])
                bank_read(bk_s, ex2)
                mu2 = act(rows[2][0:1, :], row_mu, AF.Square, after=[mu])
                var = tt(rows[0][0:1, :], rows[0][0:1, :], rows[2][0:1, :],
                         ALU.subtract, after=[ex2, mu2])
                o1 = ts(rows[0][0:1, :], rows[0][0:1, :],
                        cst[0:1, C_ONE:C_ONE + 1], cEPS1,
                        ALU.mult, ALU.add, after=[var])
                o2 = act(rows[1][0:1, :], rows[0][0:1, :], AF.Ln, after=[o1])
                o3 = act(rows[1][0:1, :], rows[1][0:1, :], AF.Exp, scale=cNEGH1,
                         after=[o2])
                o4 = tt(rows[2][0:1, :], rows[1][0:1, :], rows[1][0:1, :],
                        ALU.mult, after=[o3])
                o5 = tt(rows[2][0:1, :], rows[2][0:1, :], rows[0][0:1, :],
                        ALU.mult, after=[o4])
                o6 = ts(rows[2][0:1, :], rows[2][0:1, :], cNEGH1, c151,
                        ALU.mult, ALU.add, after=[o5])
                o7 = tt(stat_row[0:1, :], rows[1][0:1, :], rows[2][0:1, :],
                        ALU.mult, after=[o6, stat_busy[0]])
                oc = tt(stat2[0:1, :], row_mu, stat_row[0:1, :], ALU.mult,
                        after=[o7, stat2_busy[0]])
                sst = st_(bass.AP(d_stat, t * TFF, [[L, 1], [1, TFF]]),
                          stat_row[0:1, :], after=[o7], chan="lnst")
                sst2 = st_(bass.AP(d_stat, L + t * TFF, [[L, 1], [1, TFF]]),
                           stat2[0:1, :], after=[oc], chan="lnst2")
                stat_busy[0] = sst
                stat2_busy[0] = sst2
                b1 = ld(rstd_b[bi][:], bass.AP(d_stat, t * TFF, [[0, 128], [1, TFF]]),
                        after=[sst, lu_bc[bi]], chan=f"lnb{bi}")
                b2 = ld(c_b[bi][:], bass.AP(d_stat, L + t * TFF, [[0, 128], [1, TFF]]),
                        after=[sst2, lu_bc[bi]], chan=f"lnc{bi}")
                m2s = []
                for hc in range(NHC):
                    xs = x_sb[bi][:, hc * 512:(hc + 1) * 512]
                    m1 = tt(t1, xs, rstd_b[bi][:], ALU.mult,
                            after=[b1, lu_xn[bi], (m2s[-1] if m2s else None)])
                    m2 = tt(xn_sb[bi][:, hc * 512:(hc + 1) * 512], t1, c_b[bi][:],
                            ALU.subtract, after=[m1, b2])
                    m2s.append(m2)
                lu_x[bi] = m2s[-1]
                lu_bc[bi] = m2s[-1]
                outs = []
                for g in range(4):
                    dstap = bass.AP(dst_dram, g * 4 * 128 * L + t * TFF,
                                    [[L, 128], [128 * L, 4], [1, TFF]])
                    outs.append(st_(dstap, xn_sb[bi][:, g * 2048:(g + 1) * 2048],
                                    after=[m2s[-1]], chan=f"lno{bi}"))
                lu_xn[bi] = outs[-1]
                if paranoid & 1:
                    p.barrier()
            p.barrier()
            arena.reset(m0)

        def bank_read(bi_, op):
            banks.r[bi_] = op
            return op

        def ffn_phase(xn_dram, res_dram, out_dram):
            m0 = arena.mark()
            xn_T = [arena.alloc(8192, BF16) for _ in range(2)]
            hdn_T = arena.alloc(NIC * 512, BF16)
            wgb = [arena.alloc(2048, BF16) for _ in range(2)]
            wub = [arena.alloc(2048, BF16) for _ in range(2)]
            wdb = [arena.alloc(ICF, BF16) for _ in range(2)]
            gsl = [arena.alloc(512, F32) for _ in range(2)]
            usl = [arena.alloc(512, F32) for _ in range(2)]
            x0b = [arena.alloc(512, F32) for _ in range(2)]
            x1b = [arena.alloc(512, F32) for _ in range(2)]
            lu_xn = [None, None]
            lu_wg = [None, None]; lu_wu = [None, None]; lu_wd = [None, None]
            lu_gs = [None, None]; lu_us = [None, None]
            lu_x0 = [None, None]; lu_x1 = [None, None]
            hdn_reader = None
            for t in range(NTF):
                bi = t % 2
                ldx = []
                for g in range(4):
                    srcap = bass.AP(xn_dram, g * 4 * 128 * L + t * TFF,
                                    [[L, 128], [128 * L, 4], [1, TFF]])
                    ldx.append(ld(xn_T[bi][:, g * 2048:(g + 1) * 2048], srcap,
                                  after=[lu_xn[bi]], chan=f"fxn{bi}"))
                hdn_ops = []
                for ic in range(NIC):
                    wi = ic % 2
                    bg = wi * 2
                    bu = wi * 2 + 1
                    lg = ld(wgb[wi][:], bass.AP(d_wg, ic * 128 * H, [[H, 128], [1, H]]),
                            after=[lu_wg[wi]], chan=f"fwg{wi}")
                    lu_ = ld(wub[wi][:], bass.AP(d_wu, ic * 128 * H, [[H, 128], [1, H]]),
                             after=[lu_wu[wi]], chan=f"fwu{wi}")
                    gm = um = None
                    for hc in range(NHC):
                        gm = mm(bg, wgb[wi][:, hc * 128:(hc + 1) * 128],
                                xn_T[bi][:, hc * 512:(hc + 1) * 512],
                                start=(hc == 0), stop=(hc == NHC - 1),
                                after=ldx + [lg])
                        um = mm(bu, wub[wi][:, hc * 128:(hc + 1) * 128],
                                xn_T[bi][:, hc * 512:(hc + 1) * 512],
                                start=(hc == 0), stop=(hc == NHC - 1),
                                after=ldx + [lu_])
                    lu_wg[wi] = gm
                    lu_wu[wi] = um
                    eg = act(gsl[wi], banks.t[bg][:], AF.Silu,
                             scale=gs[:, ic:ic + 1], after=[gm, lu_gs[wi]])
                    bank_read(bg, eg)
                    eu = act(usl[wi], banks.t[bu][:], AF.Copy,
                             scale=us[:, ic:ic + 1], after=[um, lu_us[wi]])
                    bank_read(bu, eu)
                    hm = tt(hdn_T[:, ic * 512:(ic + 1) * 512], gsl[wi], usl[wi],
                            ALU.mult, after=[eg, eu, hdn_reader])
                    lu_gs[wi] = hm
                    lu_us[wi] = hm
                    hdn_ops.append(hm)
                lu_xn[bi] = hdn_ops[-1]
                ev_last = None
                for hc in range(NHC):
                    wi = hc % 2
                    bd = 4 + wi
                    lw = ld(wdb[wi][:], bass.AP(d_wd, hc * 128 * ICF,
                                                [[ICF, 128], [1, ICF]]),
                            after=[lu_wd[wi]], chan=f"fwd{wi}")
                    lx = ld(x0b[wi][:], bass.AP(res_dram, hc * 128 * L + t * TFF,
                                                [[L, 128], [1, TFF]]),
                            after=[lu_x0[wi]], chan=f"fx0{wi}")
                    dm = None
                    for ic in range(NIC):
                        dm = mm(bd, wdb[wi][:, ic * 128:(ic + 1) * 128],
                                hdn_T[:, ic * 512:(ic + 1) * 512],
                                start=(ic == 0), stop=(ic == NIC - 1),
                                after=[lw, hdn_ops[ic]])
                    lu_wd[wi] = dm
                    ev = stt(x1b[wi][:], banks.t[bd][:], hds[:, hc:hc + 1],
                             x0b[wi][:], ALU.mult, ALU.add,
                             after=[dm, lx, lu_x1[wi]])
                    bank_read(bd, ev)
                    lu_x0[wi] = ev
                    so = st_(bass.AP(out_dram, hc * 128 * L + t * TFF,
                                     [[L, 128], [1, TFF]]),
                             x1b[wi][:], after=[ev], chan=f"fx1{wi}")
                    lu_x1[wi] = so
                    ev_last = ev
                hdn_reader = ev_last
                if paranoid & 2:
                    p.barrier()
            p.barrier()
            arena.reset(m0)

        def pass1():
            m0 = arena.mark()
            xn2 = arena.alloc(NHC * SUP, BF16)
            winb = [arena.alloc(2048, BF16) for _ in range(2)]
            conv_in = arena.alloc(LC + 4, F32)
            acc_a = arena.alloc(512, F32)
            acc_b = arena.alloc(512, F32)
            hcv = arena.alloc(512, F32)
            hcb = arena.alloc(512, BF16)
            gat = arena.alloc(512, F32)
            ts_acc = arena.alloc(SUP, F32)
            b_acc = arena.alloc(SUP, F32)
            c_acc = arena.alloc(SUP, F32)
            rows = [arena.alloc(512, F32)[0:1, :] for _ in range(3)]
            rstd_b = arena.alloc(512, F32)
            bcn = arena.alloc(512, F32)
            tsn_f = arena.alloc(512, F32)
            lu_win = [None, None]
            lu_xn2 = None
            prev_super_done = None
            conv_free = None
            hcv_free = None
            hcb_free = None
            gat_free = None
            for s in range(NSUP):
                ldx = []
                for g in range(4):
                    srcap = bass.AP(d_xn2, g * 4 * 128 * L + s * SUP,
                                    [[L, 128], [128 * L, 4], [1, SUP]])
                    ldx.append(ld(xn2[:, g * 4 * SUP:(g + 1) * 4 * SUP], srcap,
                                  after=[lu_xn2, prev_super_done], chan="p1xn"))
                ms_t = memset(ts_acc[:], 0.0, after=[prev_super_done])
                ms_b = memset(b_acc[0:16, :], 0.0, after=[prev_super_done])
                ms_c = memset(c_acc[0:16, :], 0.0, after=[prev_super_done])
                ssm_last = [None] * C4
                for ot in range(NOC):
                    wi = ot % 2
                    is_h = ot < NKT
                    lw = ld(winb[wi][:],
                            bass.AP(w_in_t, ot * 128 * H, [[H, 128], [1, H]]),
                            after=[lu_win[wi]], q="gq", chan=f"p1w{wi}")
                    for c4 in range(C4):
                        bpj = c4 % 2
                        gchunk = s * C4 + c4
                        pm = None
                        for hc in range(NHC):
                            pm = mm(bpj, winb[wi][:, hc * 128:(hc + 1) * 128],
                                    xn2[:, hc * SUP + c4 * LC: hc * SUP + c4 * LC + LC],
                                    start=(hc == 0), stop=(hc == NHC - 1),
                                    after=ldx + [lw])
                        if c4 == C4 - 1:
                            lu_win[wi] = pm
                        if is_h:
                            if gchunk == 0:
                                hr = memset(conv_in[:, 0:3], 0.0, after=[conv_free])
                            else:
                                hr = cp(conv_in[:, 0:3], halo[:, ot * 3:ot * 3 + 3],
                                        after=[conv_free])
                            ev = act(conv_in[:, 3:3 + LC], banks.t[bpj][:],
                                     AF.Copy, after=[pm, conv_free])
                            bank_read(bpj, ev)
                            hs = cp(halo[:, ot * 3:ot * 3 + 3],
                                    conv_in[:, LC:LC + 3], after=[ev, hr])
                            c0 = ts(acc_a, conv_in[:, 0:LC],
                                    convw[:, ot * 4:ot * 4 + 1], None, ALU.mult,
                                    after=[ev, hr])
                            c1 = stt(acc_b, conv_in[:, 1:1 + LC],
                                     convw[:, ot * 4 + 1:ot * 4 + 2], acc_a,
                                     ALU.mult, ALU.add, after=[c0])
                            c2 = stt(acc_a, conv_in[:, 2:2 + LC],
                                     convw[:, ot * 4 + 2:ot * 4 + 3], acc_b,
                                     ALU.mult, ALU.add, after=[c1])
                            c3 = stt(acc_b, conv_in[:, 3:3 + LC],
                                     convw[:, ot * 4 + 3:ot * 4 + 4], acc_a,
                                     ALU.mult, ALU.add, after=[c2])
                            conv_free = c3
                            sl = act(hcv, acc_b, AF.Silu,
                                     bias=convb[:, ot:ot + 1],
                                     after=[c3, hcv_free])
                            cb = cp(hcb, hcv, after=[sl, hcb_free])
                            so = st_(bass.AP(d_hconv, ot * 128 * L + gchunk * LC,
                                             [[L, 128], [1, LC]]), hcv, after=[sl],
                                     chan="p1h")
                            hcv_free = [cb, so]
                            bts, bbb, bcc = 2, 3, 4
                            m1 = mm(bts, wx_bf[:, ot * 160:ot * 160 + 128], hcb,
                                    start=True, stop=True, after=[cb])
                            m2 = mm(bbb, wx_bf[:, ot * 160 + 128:ot * 160 + 144],
                                    hcb, start=True, stop=True, after=[cb], m=16)
                            m3 = mm(bcc, wx_bf[:, ot * 160 + 144:ot * 160 + 160],
                                    hcb, start=True, stop=True, after=[cb], m=16)
                            hcb_free = m3
                            a1 = tt(ts_acc[:, c4 * LC:(c4 + 1) * LC],
                                    ts_acc[:, c4 * LC:(c4 + 1) * LC],
                                    banks.t[bts][:], ALU.add,
                                    after=[m1, ms_t, ssm_last[c4]])
                            bank_read(bts, a1)
                            a2 = tt(b_acc[0:16, c4 * LC:(c4 + 1) * LC],
                                    b_acc[0:16, c4 * LC:(c4 + 1) * LC],
                                    banks.t[bbb][0:16, :], ALU.add,
                                    after=[m2, ms_b, a1])
                            bank_read(bbb, a2)
                            a3 = tt(c_acc[0:16, c4 * LC:(c4 + 1) * LC],
                                    c_acc[0:16, c4 * LC:(c4 + 1) * LC],
                                    banks.t[bcc][0:16, :], ALU.add,
                                    after=[m3, ms_c, a2])
                            bank_read(bcc, a3)
                            ssm_last[c4] = a3
                        else:
                            ev = act(gat, banks.t[bpj][:], AF.Silu,
                                     after=[pm, gat_free])
                            bank_read(bpj, ev)
                            so = st_(bass.AP(d_gate,
                                             (ot - NKT) * 128 * L + gchunk * LC,
                                             [[L, 128], [1, LC]]), gat, after=[ev],
                                     chan="p1g")
                            gat_free = so
                lu_xn2 = p.last_real["tensor"]
                rms_last = None
                last_pe_read = None
                for c4 in range(C4):
                    gchunk = s * C4 + c4
                    tss = ts_acc[:, c4 * LC:(c4 + 1) * LC]
                    bk = 6
                    sq = act(conv_in[:, 0:LC], tss, AF.Square,
                             after=[ssm_last[c4], rms_last, conv_free])
                    sm = mm(bk, ones[:, 0:1], conv_in[:, 0:LC], start=True,
                            stop=True, after=[sq], m=1)
                    r7 = rstd_chain(banks.t[bk][0:1, :], cI128, rows, LC,
                                    after=[sm])
                    bank_read(bk, r7)
                    ss = st_(bass.AP(d_stat, gchunk * LC, [[L, 1], [1, LC]]),
                             stat_row[0:1, :], after=[r7], chan="p1st")
                    stat_busy[0] = ss
                    bb = ld(rstd_b[:], bass.AP(d_stat, gchunk * LC,
                                               [[0, 128], [1, LC]]), after=[ss],
                            chan="p1b")
                    t1_ = tt(tsn_f, tss, rstd_b[:], ALU.mult, after=[bb, sq])
                    t2_ = ts(ts_n[:, gchunk * LC:(gchunk + 1) * LC], tsn_f,
                             dtln[:, 0:1], None, ALU.mult, after=[t1_])
                    prev = t2_
                    for half, w_ln, srow in ((0, bln, 1), (1, cln, 2)):
                        acc_src = b_acc if half == 0 else c_acc
                        bcx = acc_src[0:16, c4 * LC:(c4 + 1) * LC]
                        sq2 = act(conv_in[0:16, 0:LC], bcx, AF.Square,
                                  after=[prev, ssm_last[c4], sm])
                        sm2 = mm(bk, ones[0:16, 0:1], conv_in[0:16, 0:LC],
                                 start=True, stop=True, after=[sq2], m=1)
                        last_pe_read = sm2
                        r72 = rstd_chain(banks.t[bk][0:1, :], cI16, rows, LC,
                                         after=[sm2])
                        bank_read(bk, r72)
                        ss2 = st_(bass.AP(d_stat, srow * L + gchunk * LC,
                                          [[L, 1], [1, LC]]),
                                  stat_row[0:1, :], after=[r72],
                                  chan=f"p1st{srow}")
                        stat_busy[0] = ss2
                        bb2 = ld(rstd_b[0:16, :],
                                 bass.AP(d_stat, srow * L + gchunk * LC,
                                         [[0, 16], [1, LC]]), after=[ss2],
                                 chan="p1b2")
                        n1 = tt(bcn[0:16, :], bcx, rstd_b[0:16, :], ALU.mult,
                                after=[bb2, sq2])
                        n2 = ts(bcn[0:16, :], bcn[0:16, :], w_ln[0:16, 0:1],
                                None, ALU.mult, after=[n1])
                        so2 = st_(bass.AP(d_bc, half * 16 * L + gchunk * LC,
                                          [[L, 16], [1, LC]]),
                                  bcn[0:16, :], after=[n2], chan="p1bc")
                        prev = so2
                    rms_last = prev
                conv_free = last_pe_read
                prev_super_done = rms_last
                if paranoid & 4:
                    p.barrier()
            p.barrier()
            arena.reset(m0)

        def pass2():
            m0 = arena.mark()
            hcv = [arena.alloc(KB * 512, F32) for _ in range(2)]
            gat = [arena.alloc(KB * 512, F32) for _ in range(2)]
            delta = arena.alloc(KB * 512, F32)
            du = arena.alloc(KB * 512, F32)
            yb = arena.alloc(KB * 512, F32)
            dab = [arena.alloc(LC + 1, F32) for _ in range(2)]
            dbb = [arena.alloc(LC + 1, F32) for _ in range(2)]
            bcb = [arena.alloc(512, F32) for _ in range(2)]
            ccb = [arena.alloc(512, F32) for _ in range(2)]
            ymt_d = arena.alloc(512, F32)
            ymt_p = arena.alloc(512, F32)
            tmp_sp = [arena.alloc(512, F32) for _ in range(2)]
            x1b = [arena.alloc(512, F32) for _ in range(2)]
            x2b = [arena.alloc(512, F32) for _ in range(2)]
            y_g = arena.alloc(NKT * 512, BF16)
            woutb = [arena.alloc(I, BF16) for _ in range(2)]
            pad_ok = [memset(dab[0][:, LC:LC + 1], 1.0),
                      memset(dab[1][:, LC:LC + 1], 1.0),
                      memset(dbb[0][:, LC:LC + 1], 0.0),
                      memset(dbb[1][:, LC:LC + 1], 0.0)]
            lu_h = [None, None]; lu_g = [None, None]
            lu_bc = [None, None]; lu_cc = [None, None]
            lu_w = [None, None]; lu_x1 = [None, None]; lu_x2 = [None, None]
            yg_last = {"dve": None, "pool": None}
            wout_done = None
            delta_busy = None
            du_busy = None
            tmp_busy = [None, None]
            st_carry = None
            scan_carry = {}
            sring_v = s_ring[:].rearrange("p (k l) -> p k l", l=LC + 1)
            state_v = state[:].rearrange("p (k n) -> p k n", n=NN)
            for c in range(NCH):
                for kb in range(NKB):
                    bi = kb % 2
                    d1_ops = []
                    y0_ops = []
                    lg_ops = []
                    delta_ops = []
                    for kk in range(KB):
                        k = kb * KB + kk
                        lh = ld(hcv[bi][:, kk * 512:kk * 512 + LC],
                                bass.AP(d_hconv, k * 128 * L + c * LC,
                                        [[L, 128], [1, LC]]),
                                after=[lu_h[bi]], chan=f"p2h{bi}")
                        lg = ld(gat[bi][:, kk * 512:kk * 512 + LC],
                                bass.AP(d_gate, k * 128 * L + c * LC,
                                        [[L, 128], [1, LC]]),
                                after=[lu_g[bi]], chan=f"p2g{bi}")
                        bdt = 6 + (kk % 2)
                        dmm = mm(bdt, wdt_bf[:, k * 128:(k + 1) * 128],
                                 ts_n[:, c * LC:(c + 1) * LC], start=True,
                                 stop=True, after=[])
                        dl = delta[:, kk * 512:kk * 512 + LC]
                        sp1 = act(tmp_sp[kk % 2], banks.t[bdt][:, 0:LC],
                                  AF.Abs, bias=dtb[:, k:k + 1],
                                  after=[dmm, tmp_busy[kk % 2]])
                        sp2 = act(tmp_sp[kk % 2], tmp_sp[kk % 2], AF.Exp,
                                  scale=cst[:, C_NEG1:C_NEG1 + 1], after=[sp1])
                        sp3 = act(tmp_sp[kk % 2], tmp_sp[kk % 2], AF.Ln,
                                  bias=cst[:, C_ONE:C_ONE + 1], after=[sp2])
                        sp4 = act(dl, banks.t[bdt][:, 0:LC], AF.Relu,
                                  bias=dtb[:, k:k + 1],
                                  after=[sp3, delta_busy])
                        bank_read(bdt, sp4)
                        sp5 = tt(dl, dl, tmp_sp[kk % 2], ALU.add,
                                 after=[sp4, delta_busy])
                        tmp_busy[kk % 2] = sp5
                        delta_ops.append(sp5)
                        d1 = tt(du[:, kk * 512:kk * 512 + LC], dl,
                                hcv[bi][:, kk * 512:kk * 512 + LC], ALU.mult,
                                after=[sp5, lh, du_busy])
                        y0 = ts(yb[:, kk * 512:kk * 512 + LC],
                                hcv[bi][:, kk * 512:kk * 512 + LC],
                                D_sb[:, k:k + 1], None, ALU.mult,
                                after=[lh, yg_last["dve"], yg_last["pool"]])
                        d1_ops.append(d1)
                        y0_ops.append(y0)
                        lg_ops.append(lg)
                    lu_h[bi] = [d1_ops[-1], y0_ops[-1]]
                    scan_prev = dict(scan_carry)
                    yop_prev = dict(enumerate(y0_ops))
                    st_prev = st_carry
                    last_ex = None
                    for n in range(NN):
                        nbi = n % 2
                        lb = ld(bcb[nbi][:],
                                bass.AP(d_bc, n * L + c * LC, [[0, 128], [1, LC]]),
                                after=[lu_bc[nbi]], chan=f"p2b{nbi}")
                        lc_ = ld(ccb[nbi][:],
                                 bass.AP(d_bc, (16 + n) * L + c * LC,
                                         [[0, 128], [1, LC]]),
                                 after=[lu_cc[nbi]], chan=f"p2c{nbi}")
                        scan_ops = []
                        for kk in range(KB):
                            k = kb * KB + kk
                            abi = kk % 2
                            dl = delta[:, kk * 512:kk * 512 + LC]
                            ex = act(dab[abi][:, 0:LC], dl, AF.Exp,
                                     scale=A_sb[:, k * NN + n:k * NN + n + 1],
                                     after=[scan_prev.get(abi), delta_ops[kk]]
                                     + pad_ok)
                            last_ex = ex
                            db_ = tt(dbb[abi][:, 0:LC],
                                     du[:, kk * 512:kk * 512 + LC],
                                     bcb[nbi][:], ALU.mult,
                                     after=[lb, scan_prev.get(abi), d1_ops[kk]])
                            sc = scan(sring_v[:, kk, :], dab[abi][:],
                                      dbb[abi][:],
                                      state[:, k * NN + n:k * NN + n + 1],
                                      after=[ex, db_, yop_prev.get(kk), st_prev])
                            scan_prev[abi] = sc
                            scan_ops.append(sc)
                            yeng = "dve" if kk < 2 else "pool"
                            ymt = ymt_d if kk < 2 else ymt_p
                            y1 = tt(ymt, sring_v[:, kk, 0:LC], ccb[nbi][:],
                                    ALU.mult,
                                    after=[sc, lc_, yg_last[yeng]], eng=yeng)
                            y2 = tt(yb[:, kk * 512:kk * 512 + LC],
                                    yb[:, kk * 512:kk * 512 + LC], ymt,
                                    ALU.add, after=[y1, yop_prev.get(kk)],
                                    eng=yeng)
                            yop_prev[kk] = y2
                            yg_last[yeng] = y2
                        lu_bc[nbi] = scan_ops[-1]
                        lu_cc[nbi] = [yop_prev[1], yop_prev[KB - 1]]
                        st_prev = act(state_v[:, kb * KB:(kb + 1) * KB, n],
                                      sring_v[:, :, LC], AF.Copy,
                                      after=scan_ops)
                    st_carry = st_prev
                    scan_carry = dict(scan_prev)
                    delta_busy = [last_ex, d1_ops[-1]]
                    du_busy = scan_ops[-1]
                    for kk in range(KB):
                        k = kb * KB + kk
                        yeng = "dve" if kk % 2 == 0 else "pool"
                        yg = tt(y_g[:, k * 512:k * 512 + LC],
                                yb[:, kk * 512:kk * 512 + LC],
                                gat[bi][:, kk * 512:kk * 512 + LC], ALU.mult,
                                after=[yop_prev[kk], st_prev, lg_ops[kk],
                                       yg_last[yeng], wout_done], eng=yeng)
                        yg_last[yeng] = yg
                    lu_g[bi] = [yg_last["dve"], yg_last["pool"]]
                wev = None
                for ot in range(NHC):
                    wi = ot % 2
                    bw = 2 + wi
                    lw = ld(woutb[wi][:],
                            bass.AP(d_wout, ot * 128 * I, [[I, 128], [1, I]]),
                            after=[lu_w[wi]], chan=f"p2w{wi}")
                    lx = ld(x1b[wi][:], bass.AP(d_x1, ot * 128 * L + c * LC,
                                                [[L, 128], [1, LC]]),
                            after=[lu_x1[wi]], chan=f"p2x1{wi}")
                    wm = None
                    for ic in range(NKT):
                        wm = mm(bw, woutb[wi][:, ic * 128:(ic + 1) * 128],
                                y_g[:, ic * 512:ic * 512 + LC],
                                start=(ic == 0), stop=(ic == NKT - 1),
                                after=[lw, yg_last["dve"], yg_last["pool"]])
                    lu_w[wi] = wm
                    ev = tt(x2b[wi][:], banks.t[bw][:, 0:LC], x1b[wi][:],
                            ALU.add, after=[wm, lx, lu_x2[wi]])
                    bank_read(bw, ev)
                    lu_x1[wi] = ev
                    so = st_(bass.AP(d_x2, ot * 128 * L + c * LC,
                                     [[L, 128], [1, LC]]), x2b[wi][:], after=[ev],
                             chan=f"p2x2{wi}")
                    lu_x2[wi] = so
                    wev = ev
                wout_done = wev
                if paranoid & 8:
                    p.barrier()
            p.barrier()
            arena.reset(m0)

        ln_phase(xT, d_xn1)
        ffn_phase(d_xn1, xT, d_x1)
        ln_phase(d_x1, d_xn2)
        pass1()
        pass2()
        ln_phase(d_x2, d_xn3)
        ffn_phase(d_xn3, d_x2, o_out)

        if debug:
            p.barrier()
            dbg_map = {"d_xn1": (d_xn1, [H, L], BF16), "d_x1": (d_x1, [H, L], F32),
                       "d_xn2": (d_xn2, [H, L], BF16),
                       "d_hconv": (d_hconv, [I, L], F32),
                       "d_gate": (d_gate, [I, L], F32),
                       "d_bc": (d_bc, [32, L], F32),
                       "d_x2": (d_x2, [H, L], F32),
                       "d_xn3": (d_xn3, [H, L], BF16),
                       "d_stat": (d_stat, [4, L], F32)}
            for wname, wt in (("d_wg", d_wg), ("d_wu", d_wu)):
                dbgw = nc.dram_tensor("dbg_" + wname, [8, 128, H], BF16,
                                      kind="ExternalOutput")
                ld(dbgw.ap(), bass.AP(wt, 0, [[1, 8 * 128 * H]]), chan="dbg")
            for name, (src, shape, dt_) in dbg_map.items():
                dbg = nc.dram_tensor("dbg_" + name, shape, dt_,
                                     kind="ExternalOutput")
                ld(dbg.ap(), src.ap(), chan="dbg")
            dbg_tsn = nc.dram_tensor("dbg_tsn", [128, L], BF16,
                                     kind="ExternalOutput")
            dbg_state = nc.dram_tensor("dbg_state", [128, NKT * NN], F32,
                                       kind="ExternalOutput")
            ld(dbg_tsn.ap(), ts_n[:], chan="dbg")
            ld(dbg_state.ap(), state[:], chan="dbg")

        p.do("sq", None, after=list(p.chan_last.values()))

        per_stream, keys = p.finalize()
        sems = {k: ctx.enter_context(nc.semaphore(f"sem_{k}")) for k in keys}
        hw = {}
        block = ctx.enter_context(nc.Block())

        @block.sync
        def _(sync):
            Prog.emit_stream(sync, "sync", per_stream["sync"], sems, hw)

        @block.tensor
        def _(tensor):
            Prog.emit_stream(tensor, "tensor", per_stream["tensor"], sems, hw)

        @block.scalar
        def _(scalar):
            Prog.emit_stream(scalar, "scalar", per_stream["scalar"], sems, hw)

        @block.vector
        def _(vector):
            Prog.emit_stream(vector, "vector", per_stream["vector"], sems, hw)

        @block.gpsimd
        def _(gpsimd):
            Prog.emit_stream(gpsimd, "gpsimd", per_stream["gpsimd"], sems, hw)

    return nc


def _prep_shared(inp):
    W_in = inp["L0_attn_W_in"]
    Wx = inp["L0_attn_Wx"]
    Wdt = inp["L0_attn_Wdt"]
    Wout = inp["L0_attn_Wout"]
    gw = inp["L0_ffn_gate_w"]
    uw = inp["L0_ffn_up_w"]
    dw = inp["L0_ffn_down_w"]
    d = {}
    d["w_in_t"] = np.ascontiguousarray(
        np.asarray(W_in).reshape(NOC, 128, NHC, 128).transpose(0, 3, 2, 1)
        .reshape(NOC, 128, H))
    d["wx_t"] = np.ascontiguousarray(np.asarray(Wx).T.reshape(NKT, 128, 160))
    d["wdt_t"] = np.ascontiguousarray(np.asarray(Wdt).T)
    d["wout_t"] = np.ascontiguousarray(
        np.asarray(Wout).reshape(NHC, 128, NKT, 128).transpose(0, 3, 2, 1)
        .reshape(NHC, 128, I))
    d["wg_t"] = np.ascontiguousarray(
        np.asarray(gw).reshape(NIC, 128, NHC, 128).transpose(0, 3, 2, 1)
        .reshape(NIC, 128, H))
    d["wu_t"] = np.ascontiguousarray(
        np.asarray(uw).reshape(NIC, 128, NHC, 128).transpose(0, 3, 2, 1)
        .reshape(NIC, 128, H))
    d["wd_t"] = np.ascontiguousarray(
        np.asarray(dw).reshape(NHC, 128, NIC, 128).transpose(0, 3, 2, 1)
        .reshape(NHC, 128, ICF))
    d["convw_s"] = np.ascontiguousarray(
        np.asarray(inp["L0_attn_conv_w"]).reshape(32, 128, 4)
        .transpose(1, 0, 2).reshape(128, 128))
    d["convb_s"] = np.ascontiguousarray(
        np.asarray(inp["L0_attn_conv_b"]).reshape(32, 128).T)
    d["dtb_s"] = np.ascontiguousarray(
        np.asarray(inp["L0_attn_dt_bias"]).reshape(32, 128).T)
    d["A_s"] = np.ascontiguousarray(
        np.asarray(inp["L0_attn_A"]).reshape(32, 128, 16)
        .transpose(1, 0, 2).reshape(128, 512))
    d["D_s"] = np.ascontiguousarray(np.asarray(inp["L0_attn_D"]).reshape(32, 128).T)
    d["gs_s"] = np.ascontiguousarray(
        np.asarray(inp["L0_ffn_gate_s"]).reshape(NIC, 128).T)
    d["us_s"] = np.ascontiguousarray(
        np.asarray(inp["L0_ffn_up_s"]).reshape(NIC, 128).T)
    d["ds_s"] = np.ascontiguousarray(
        np.asarray(inp["L0_ffn_down_s"]).reshape(NHC, 128).T)
    d["dtln_s"] = np.ascontiguousarray(
        np.asarray(inp["L0_attn_dt_ln_w"]).reshape(128, 1))
    d["bln_s"] = np.ascontiguousarray(
        np.asarray(inp["L0_attn_b_ln_w"]).reshape(16, 1))
    d["cln_s"] = np.ascontiguousarray(
        np.asarray(inp["L0_attn_c_ln_w"]).reshape(16, 1))
    cvals = np.zeros(NCST, np.float32)
    cvals[C_EPS] = EPS
    cvals[C_ONE] = 1.0
    cvals[C_NEG1] = -1.0
    cvals[C_NEGH] = -0.5
    cvals[C_15] = 1.5
    cvals[C_HALF] = 0.5
    cvals[C_I2048] = 1.0 / 2048.0
    cvals[C_I128] = 1.0 / 128.0
    cvals[C_I16] = 1.0 / 16.0
    d["cst_in"] = np.ascontiguousarray(
        np.broadcast_to(cvals.astype(np.float32), (128, NCST)))
    return d


def kernel(**inputs):
    shared = _prep_shared(inputs)
    x = np.asarray(inputs["L0_attn_x"])
    nc = build_nc()
    in_maps = []
    for core in range(8):
        b = core // 2
        m = dict(shared)
        m["xT"] = np.ascontiguousarray(x[b].T)
        in_maps.append(m)
    res = run_bass_kernel_spmd(nc, in_maps, core_ids=list(range(8)))
    out = np.empty((B, L, H), np.float32)
    for b in range(B):
        out[b] = np.asarray(res.results[2 * b]["o_out"]).T
    return out
